# version 04 (final)
# baseline (speedup 1.0000x reference)
import contextlib
import math

import numpy as np
import ml_dtypes

import concourse.bass as bass
import concourse.mybir as mybir
from concourse.bass_utils import run_bass_kernel_spmd

F32 = mybir.dt.float32
BF16 = mybir.dt.bfloat16
I32 = mybir.dt.int32
AF = mybir.ActivationFunctionType
ALU = mybir.AluOpType
AX = mybir.AxisListType
bf16 = ml_dtypes.bfloat16

D = 4096
T = 2048
DKV = 1024
HD = 128
NH = 32
NKV = 8
II = 11008
IHALF = II // 2
NIC = IHALF // 128
NCH = D // 128
OWN = 1024
EPS = 1e-5
NW = 6

MAP0 = 16512
OFF_A = MAP0
OFF_WG = MAP0 + 960
OFF_KT = MAP0 + 7168
OFF_V = OFF_KT + 32768
OFF_TB = OFF_V + 32768
OFF_D = OFF_TB + 8192
OFF_E = OFF_D + 32768
OFF_F = OFF_E + 32768
OFF_G = OFF_F + 32768


def build_program(debug=(), only_q0=False, norope=False, xnT_direct=False, no_bcast=False, upto=None):
    debug = set(debug)
    nc = bass.Bass("TRN2", target_bir_lowering=False)

    x_full = nc.dram_tensor("x_full", [T, D], F32, kind="ExternalInput").ap()
    x_q = nc.dram_tensor("x_q", [OWN, D], F32, kind="ExternalInput").ap()
    wq = nc.dram_tensor("wq", [D, D], F32, kind="ExternalInput").ap()
    wk = nc.dram_tensor("wk", [DKV, D], F32, kind="ExternalInput").ap()
    wv = nc.dram_tensor("wv", [DKV, D], F32, kind="ExternalInput").ap()
    wo = nc.dram_tensor("wo", [D, D], F32, kind="ExternalInput").ap()
    gw = nc.dram_tensor("gw", [II, D], I32, kind="ExternalInput").ap()
    uw = nc.dram_tensor("uw", [II, D], I32, kind="ExternalInput").ap()
    dw = nc.dram_tensor("dw", [D, II], I32, kind="ExternalInput").ap()
    gs_in = nc.dram_tensor("gs_in", [1, 1], F32, kind="ExternalInput").ap()
    us_in = nc.dram_tensor("us_in", [1, 1], F32, kind="ExternalInput").ap()
    ds_in = nc.dram_tensor("ds_in", [1, 1], F32, kind="ExternalInput").ap()
    kcos_in = nc.dram_tensor("kcos_in", [128, T], BF16, kind="ExternalInput").ap()
    ksin_in = nc.dram_tensor("ksin_in", [128, T], BF16, kind="ExternalInput").ap()
    qcos_in = nc.dram_tensor("qcos_in", [128, OWN], BF16, kind="ExternalInput").ap()
    qsin_in = nc.dram_tensor("qsin_in", [128, OWN], BF16, kind="ExternalInput").ap()
    masks_in = nc.dram_tensor("masks_in", [128, 8 * 256], F32, kind="ExternalInput").ap()
    ident_in = nc.dram_tensor("ident_in", [128, 128], BF16, kind="ExternalInput").ap()
    swp_in = nc.dram_tensor("swp_in", [128, 128], BF16, kind="ExternalInput").ap()
    xnT_in = (nc.dram_tensor("xnT_in", [128, NCH * 512], BF16,
                             kind="ExternalInput").ap() if xnT_direct else None)

    y = nc.dram_tensor("y", [OWN, D], F32, kind="ExternalOutput").ap()

    s_wq = nc.dram_tensor("s_wq", [D, D], BF16, kind="Internal").ap()
    s_wk = nc.dram_tensor("s_wk", [DKV, D], BF16, kind="Internal").ap()
    s_wv = nc.dram_tensor("s_wv", [DKV, D], BF16, kind="Internal").ap()
    s_wo = nc.dram_tensor("s_wo", [D, D], BF16, kind="Internal").ap()
    s_gw = nc.dram_tensor("s_gw", [II, D], BF16, kind="Internal").ap()
    s_uw = nc.dram_tensor("s_uw", [II, D], BF16, kind="Internal").ap()
    s_dw = nc.dram_tensor("s_dw", [D, II], BF16, kind="Internal").ap()
    rnT_sp = nc.dram_tensor("rnT_sp", [128, NCH * OWN], BF16, kind="Internal").ap()
    d1_sp = nc.dram_tensor("d1_sp", [128, NCH * OWN], BF16, kind="Internal").ap()

    dbg = {}

    def tap(name, shape, dtype=BF16):
        if name in debug:
            dbg[name] = nc.dram_tensor("dbg_" + name, shape, dtype,
                                       kind="ExternalOutput").ap()
    tap("xnT0", [128, NCH * 512])
    tap("KT", [128, NKV * T])
    tap("V", [128, 16 * DKV])
    tap("QT0", [128, NH * 512])
    tap("attnT0", [128, NH * 512])
    tap("rnT", [128, NCH * OWN])
    tap("gu0", [128, NIC * OWN])
    tap("Qraw", [128, 4 * 512])
    tap("swq", [128, 4 * 4096])

    with contextlib.ExitStack() as ctx:
        ec = ctx.enter_context

        def sb(name, shape, dtype, off):
            return nc.alloc_sbuf_tensor_at(name, shape, dtype, offset=off).ap()

        ident = sb("ident", [128, 128], BF16, OFF_A)
        swp = sb("swp", [128, 128], BF16, OFF_G + 32768)
        ones1 = sb("ones1", [1, 128], F32, OFF_A + 256)
        sc_sb = sb("sc_sb", [1, 4], F32, OFF_A + 768)
        gs_b = sb("gs_b", [128, 1], F32, OFF_A + 800)
        us_b = sb("us_b", [128, 1], F32, OFF_A + 832)
        ds_b = sb("ds_b", [128, 1], F32, OFF_A + 864)
        stats = sb("stats", [128, 16], F32, OFF_A + 896)
        nm2 = stats[:, 0:2]
        sm2 = stats[:, 2:4]
        rr2 = stats[:, 4:6]
        ln_s = stats[:, 6:7]
        ln_ss = stats[:, 7:8]
        ln_nm = stats[:, 8:9]
        ln_v = stats[:, 9:10]
        ln_rs = stats[:, 10:11]
        ln_t = stats[:, 11:12]
        eps_b = stats[:, 12:13]
        wg = [sb(f"wg{i}", [128, 512], BF16, OFF_WG + i * 1024) for i in range(NW)]

        KT = sb("KT", [128, NKV * T], BF16, OFF_KT)
        Vt = sb("Vt", [128, 16 * DKV], BF16, OFF_V)
        kcos = sb("kcos", [128, T], BF16, OFF_TB)
        ksin = sb("ksin", [128, T], BF16, OFF_TB + 4096)
        QT = sb("QT", [128, NH * 512], BF16, OFF_D)
        xnT = sb("xnT", [128, NCH * 512], BF16, OFF_E)
        ownT = sb("ownT", [128, NCH * 512], BF16, OFF_F)
        x_t = sb("x_t", [128, D], F32, OFF_G)
        xn_t = sb("xn_t", [128, D], BF16, OFF_G + 16384)
        rope_tmp = xn_t.bitcast(F32)

        masksB = sb("masksB", [128, 8 * 256], F32, OFF_G)
        qcosB = sb("qcosB", [128, OWN], BF16, OFF_G + 8192)
        qsinB = sb("qsinB", [128, OWN], BF16, OFF_G + 10240)
        ropeQ = sb("ropeQ", [128, 1024], F32, OFF_G + 12288)
        p0B = sb("p0B", [128, 2 * T], BF16, OFF_F)
        pB = sb("pB", [128, 2 * T], BF16, OFF_F + 8192)
        ptB = sb("ptB", [128, 2 * T], BF16, OFF_F + 16384)
        attnT = sb("attnT", [128, NH * 512], BF16, OFF_E)

        poC = sb("poC", [128, NCH * 512], BF16, OFF_F)
        rnTs = sb("rnTs", [128, NCH * 512], BF16, OFF_D)
        xqC = sb("xqC", [128, D], F32, OFF_G)
        ptokC = sb("ptokC", [128, D], F32, OFF_G + 16384)
        rnC = sb("rnC", [128, D], BF16, OFF_G)

        rnT = sb("rnT", [128, NCH * OWN], BF16, OFF_KT)
        gu = sb("gu", [128, NIC * OWN], BF16, OFF_D)
        ffnT = sb("ffnT", [128, NCH * OWN], BF16, OFF_KT)
        d1b = [sb(f"d1b{i}", [128, 512], BF16, OFF_G + i * 1024) for i in range(4)]
        rF = sb("rF", [128, D], F32, OFF_G)
        yF = sb("yF", [128, D], F32, OFF_G + 16384)

        PS = nc.alloc_psum_tensor("PS", [128, 4096], F32).ap()

        def bank(i):
            return PS[:, i * 512:(i + 1) * 512]

        def bank_bf(i):
            return PS[:, i * 512:(i + 1) * 512].bitcast(BF16)

        sems = {}
        for s in (["SG", "LX", "MX", "PR", "SP2", "RB", "ST", "PE", "DV", "AC"]
                  + [f"SQ{i}" for i in range(NW)]):
            sems[s] = ec(nc.semaphore(s))
        block = ec(nc.Block())
        C = {k: 0 for k in sems}
        squse = [0] * NW
        slot_pe = [0] * NW
        tp_hist = {6: ("AC", 0), 7: ("AC", 0)}
        rope_prev = [0]

        def inc(ins, s, amt=1):
            ins.then_inc(sems[s], amt)
            C[s] += amt
            return C[s]

        def st_now():
            return [("ST", C["ST"])]

        def wt(eng, waits):
            for s, v in waits:
                if v:
                    eng.wait_ge(sems[s], v)

        STAGE_PLAN = {
            "wk": (s_wk, wk, DKV, 2), "wv": (s_wv, wv, DKV, 2),
            "wq": (s_wq, wq, D, 8), "wo": (s_wo, wo, D, 8),
            "gw": (s_gw, gw, II, 16), "uw": (s_uw, uw, II, 16),
            "dw": (s_dw, dw, D, 16),
        }

        def stage_burst(names, pe_wait=0):
            def f(g):
                if pe_wait:
                    g.wait_ge(sems["PE"], pe_wait)
                for nm in names:
                    dst, src, rows, chunks = STAGE_PLAN[nm]
                    rc = rows // chunks
                    for c in range(chunks):
                        inc(g.dma_start(out=dst[c * rc:(c + 1) * rc, :],
                                        in_=src[c * rc:(c + 1) * rc, :]),
                            "SG", 16)
            block.gpsimd(f)
            return C["SG"]

        stage_burst(["wk", "wv", "wq"])

        def f_s0(sync):
            inc(sync.dma_start(out=ident[:], in_=ident_in), "PR", 16)
            inc(sync.dma_start(out=swp[:], in_=swp_in), "PR", 16)
            inc(sync.dma_start(out=kcos[:], in_=kcos_in), "PR", 16)
            inc(sync.dma_start(out=ksin[:], in_=ksin_in), "PR", 16)
            inc(sync.dma_start(out=sc_sb[0:1, 0:1], in_=gs_in), "PR", 16)
            inc(sync.dma_start(out=sc_sb[0:1, 1:2], in_=us_in), "PR", 16)
            inc(sync.dma_start(out=sc_sb[0:1, 2:3], in_=ds_in), "PR", 16)
        block.sync(f_s0)
        ld_pro = C["PR"]

        def f_v0(v):
            v.memset(ones1[:], 1.0)
            inc(v.memset(eps_b, EPS), "DV")
        block.vector(f_v0)
        dv_ones = C["DV"]

        def f_t0(t):
            if no_bcast:
                return
            wt(t, [("PR", ld_pro), ("DV", dv_ones)])
            t.matmul(bank(0)[:, 0:1], ones1[:], sc_sb[0:1, 0:1], start=True, stop=True)
            t.matmul(bank(0)[:, 1:2], ones1[:], sc_sb[0:1, 1:2], start=True, stop=True)
            inc(t.matmul(bank(0)[:, 2:3], ones1[:], sc_sb[0:1, 2:3],
                         start=True, stop=True), "PE")
        block.tensor(f_t0)
        pe_sc = C["PE"]

        def f_a0(a):
            if no_bcast:
                return
            a.wait_ge(sems["PE"], pe_sc)
            a.copy(gs_b[:], bank(0)[:, 0:1])
            a.copy(us_b[:], bank(0)[:, 1:2])
            inc(a.copy(ds_b[:], bank(0)[:, 2:3]), "AC")
        block.scalar(f_a0)
        ac_scale = C["AC"]

        def sq_done():
            return [(f"SQ{s}", 16 * squse[s]) for s in range(NW) if squse[s]]

        def sweep(srcs, mm_fn, *, pe_waits=(), width=512):
            stage_sg = C["SG"]
            plain_waits = [(s, C[s])
                           for s in ("PR", "LX", "MX", "RB", "ST", "SP2")]
            """Transposed-read granule sweep; loads alternate LD/LD2 queues.
            mm_fn(t, i, wbuf) emits granule i's matmuls, returns last ins.
            One PE inc per granule."""
            n = len(srcs)
            pe0 = C["PE"]
            ready = [0] * n
            need_pe = [0] * n
            for i in range(n):
                slot = i % NW
                squse[slot] += 1
                ready[i] = 16 * squse[slot]
                need_pe[i] = slot_pe[slot]
                slot_pe[slot] = pe0 + i + 1

            def q_fn(sync_q):
                def f(q):
                    if not sync_q:
                        return
                    if stage_sg:
                        q.wait_ge(sems["SG"], stage_sg)
                    wt(q, plain_waits)
                    for i in range(n):
                        if need_pe[i]:
                            q.wait_ge(sems["PE"], need_pe[i])
                        inc(q.dma_start_transpose(out=wg[i % NW][:, 0:width],
                                                  in_=srcs[i]),
                            f"SQ{i % NW}", 16)
                return f
            block.sync(q_fn(True))
            block.scalar(q_fn(False))

            def t_fn(t):
                wt(t, pe_waits)
                for i in range(n):
                    t.wait_ge(sems[f"SQ{i % NW}"], ready[i])
                    inc(mm_fn(t, i, wg[i % NW]), "PE")
            block.tensor(t_fn)
            return C["PE"]

        def emit_ln(src_ap, dst_ap, *, dv_waits=(), ac_waits=()):
            def fv(v):
                v.wait_ge(sems["DV"], C["DV"])
                wt(v, dv_waits)
                inc(v.reduce_sum(ln_s, src_ap, axis=AX.X), "DV")
            block.vector(fv)
            dv_s = C["DV"]

            def fa(a):
                wt(a, ac_waits)
                wt(a, dv_waits)
                inc(a.activation(dst_ap, src_ap, AF.Square, bias=0.0, scale=1.0,
                                 accum_out=ln_ss), "AC")
            block.scalar(fa)
            ac_sq = C["AC"]

            def fv2(v):
                v.wait_ge(sems["AC"], ac_sq)
                v.wait_ge(sems["DV"], dv_s)
                inc(v.tensor_scalar_mul(ln_nm, ln_s, -1.0 / D), "DV")
                v.wait_ge(sems["DV"], C["DV"])
                inc(v.tensor_scalar_mul(ln_v, ln_ss, 1.0 / D), "DV")
                v.wait_ge(sems["DV"], C["DV"])
                inc(v.tensor_tensor(ln_t, ln_nm, ln_nm, ALU.mult), "DV")
                v.wait_ge(sems["DV"], C["DV"])
                inc(v.tensor_tensor(ln_v, ln_v, ln_t, ALU.subtract), "DV")
            block.vector(fv2)
            dv_var = C["DV"]

            def fa2(a):
                a.wait_ge(sems["DV"], dv_var)
                inc(a.activation(ln_t, ln_v, AF.Ln, bias=eps_b, scale=1.0), "AC")
                a.wait_ge(sems["AC"], C["AC"])
                inc(a.activation(ln_rs, ln_t, AF.Exp, bias=0.0, scale=-0.5), "AC")
            block.scalar(fa2)
            ac_rs = C["AC"]

            def fv3(v):
                v.wait_ge(sems["AC"], ac_rs)
                inc(v.tensor_scalar(dst_ap, src_ap, ln_nm, ln_rs,
                                    ALU.add, ALU.mult), "DV")
            block.vector(fv3)
            return C["DV"]

        def tpack(src_fn, n, dst_fn, *, pe_waits=(), evac_waits=(), b0=6):
            ngrp = (n + 7) // 8
            for grp in range(ngrp):
                cnt = min(8, n - grp * 8)
                bb = b0 + (grp % 2)

                def ft(t, grp=grp, bb=bb, cnt=cnt, bw=tp_hist[b0 + (grp % 2)]):
                    if grp == 0:
                        wt(t, pe_waits)
                    wt(t, [bw])
                    for k in range(cnt):
                        ins = t.transpose(bank_bf(bb)[:, k * 128:(k + 1) * 128],
                                          src_fn(grp * 8 + k), ident[:])
                    inc(ins, "PE")
                block.tensor(ft)
                pe_g = C["PE"]

                def fa(a, grp=grp, bb=bb, cnt=cnt, pe_g=pe_g):
                    if grp == 0:
                        wt(a, evac_waits)
                    a.wait_ge(sems["PE"], pe_g)
                    inc(a.copy(dst_fn(grp, cnt), bank_bf(bb)[:, 0:cnt * 128]),
                        "AC")
                block.scalar(fa)
                tp_hist[bb] = ("AC", C["AC"])
            return C["AC"]

        def rope_bank(bi, cos_sl, sin_sl, dst_ap, scratch, *, dv_waits=(),
                      pe_waits=(), dbg_kf=None):
            if norope:
                def fa0(a, pv=rope_prev[0]):
                    wt(a, dv_waits)
                    wt(a, pe_waits)
                    inc(a.copy(dst_ap, bank(bi)), "AC")
                block.scalar(fa0)
                def fvn(v):
                    v.wait_ge(sems["AC"], C["AC"])
                    inc(v.tensor_copy(stats[:, 14:15], stats[:, 14:15]), "DV")
                block.vector(fvn)
                rope_prev[0] = C["DV"]
                return C["DV"]
            """RoPE: psum bank bi [128,512 f32] -> dst bf16.

            DVE has no cross-partition path, so the rotate-half swap is a
            PE matmul with a +64-roll permutation (swp), applied in place
            on the bank after the original is evacuated to scratch.
            Returns DV count at which dst is done (bank free)."""
            t1 = scratch[:, 0:512]
            kf = scratch[:, 512:768].bitcast(BF16)

            def fa(a, pv=rope_prev[0]):
                wt(a, dv_waits)
                if pv:
                    a.wait_ge(sems["DV"], pv)
                wt(a, pe_waits)
                inc(a.copy(kf, bank(bi)), "AC")
            block.scalar(fa)
            ac_kf = C["AC"]
            if dbg_kf is not None:
                def fdbg(a):
                    a.wait_ge(sems["AC"], ac_kf)
                    inc(a.copy(dbg_kf, kf), "AC")
                block.scalar(fdbg)
                ac_kf = C["AC"]

            def fv0(v):
                v.wait_ge(sems["DV"], C["DV"])
                wt(v, dv_waits)
                v.wait_ge(sems["AC"], ac_kf)
                inc(v.tensor_tensor(t1, kf, cos_sl, ALU.mult), "DV")
            block.vector(fv0)
            dv_t1 = C["DV"]

            def ft(t):
                t.wait_ge(sems["AC"], ac_kf)
                t.wait_ge(sems["DV"], dv_t1)
                inc(t.matmul(bank(bi), swp[:], kf, start=True, stop=True),
                    "PE")
            block.tensor(ft)
            pe_sw = C["PE"]

            def fv(v):
                v.wait_ge(sems["PE"], pe_sw)
                v.wait_ge(sems["DV"], dv_t1)
                inc(v.tensor_tensor(kf, bank(bi), sin_sl, ALU.mult), "DV")
                v.wait_ge(sems["DV"], C["DV"])
                inc(v.tensor_tensor(dst_ap, t1, kf, ALU.add), "DV")
            block.vector(fv)
            rope_prev[0] = C["DV"]
            return C["DV"]

        def dbg_dump(name, src_ap, waits):
            if name in dbg:
                def fs(sync):
                    wt(sync, waits)
                    inc(sync.dma_start(out=dbg[name], in_=src_ap), "ST", 16)
                block.sync(fs)

        xnT3 = xnT[:].rearrange("p (c w) -> p c w", w=512)
        ownT3 = ownT[:].rearrange("p (c w) -> p c w", w=512)
        poC3 = poC[:].rearrange("p (c w) -> p c w", w=512)
        rnTs3 = rnTs[:].rearrange("p (c w) -> p c w", w=512)

        last_xn_pe = 0
        x_free = []
        banks_free = [("AC", ac_scale)]
        rope_free = []
        kv_ready = {}
        wo_done_pe = 0
        rn_spilled = {}
        r_write_st = 0
        c_done = []

        for half in range(2):
            strip = half
            for qq in (2 * half, 2 * half + 1):
                if xnT_direct:
                    def f_xd(sync):
                        inc(sync.dma_start(out=xnT[:], in_=xnT_in), "LX", 16)
                    block.sync(f_xd)
                    def f_xd2(a):
                        a.wait_ge(sems["LX"], C["LX"])
                        inc(a.copy(stats[:, 15:16], stats[:, 15:16]), "AC")
                    block.scalar(f_xd2)
                    xnT_ready_ac = C["AC"]
                for it in ([] if xnT_direct else range(4)):
                    tt = 4 * qq + it

                    def fs(sync, tt=tt, xw=list(x_free)):
                        wt(sync, xw)
                        wt(sync, sq_done())
                        inc(sync.dma_start(
                            out=x_t[:], in_=x_full[tt * 128:(tt + 1) * 128, :]),
                            "LX", 16)
                    block.sync(fs)
                    ld_x = C["LX"]

                    dv_xn = emit_ln(
                        x_t[:], xn_t[:],
                        dv_waits=[("LX", ld_x)],
                        ac_waits=[("PE", last_xn_pe)] + list(rope_free))
                    x_free = [("DV", C["DV"]), ("AC", C["AC"])]

                    tpack(lambda k: xn_t[:, k * 128:(k + 1) * 128], NCH,
                          lambda grp, cnt, it=it: xnT3[:, grp * 8:grp * 8 + cnt,
                                                       it * 128:(it + 1) * 128],
                          pe_waits=[("DV", dv_xn)] + list(banks_free),
                          evac_waits=[("PE", wo_done_pe)] + st_now())
                    last_xn_pe = C["PE"]
                xnT_ready_ac = C["AC"]

                if qq == 0:
                    dbg_dump("xnT0", xnT[:], [("AC", xnT_ready_ac)])

                srcs = [s_wk[og * 512:(og + 1) * 512, d * 128:(d + 1) * 128]
                        for og in range(2) for d in range(NCH)]

                def k_mm(t, i, w):
                    og, d = divmod(i, NCH)
                    for oc in range(4):
                        ins = t.matmul(bank(og * 4 + oc),
                                       w[:, oc * 128:(oc + 1) * 128],
                                       xnT[:, d * 512:(d + 1) * 512],
                                       start=(d == 0), stop=(d == NCH - 1))
                    return ins
                pe0_k = C["PE"]
                sweep(srcs, k_mm,
                      pe_waits=[("AC", xnT_ready_ac)] + list(banks_free))
                tc = slice(qq * 512, qq * 512 + 512)
                k_rope_dv = 0
                for og in range(2):
                    for oc in range(4):
                        h = og * 4 + oc
                        k_rope_dv = rope_bank(
                            og * 4 + oc, kcos[:, tc], ksin[:, tc],
                            KT[:, h * T + qq * 512: h * T + qq * 512 + 512],
                            rope_tmp,
                            dv_waits=([("PE", pe0_k + NCH * (og + 1)),
                                       ("PR", ld_pro), ("AC", dict(x_free).get("AC", 0))]
                                      if oc == 0 else []))
                rope_free = [("DV", k_rope_dv)]

                srcs = [s_wv[oh * 512:(oh + 1) * 512, d * 128:(d + 1) * 128]
                        for d in range(NCH) for oh in range(2)]

                def v_mm(t, i, w):
                    dd, oh = divmod(i, 2)
                    for tl in range(4):
                        ins = t.matmul(bank(tl * 2 + oh),
                                       xnT[:, dd * 512 + tl * 128:
                                           dd * 512 + (tl + 1) * 128],
                                       w[:, 0:512],
                                       start=(dd == 0), stop=(dd == NCH - 1))
                    return ins
                pe_v = sweep(srcs, v_mm,
                             pe_waits=[("DV", k_rope_dv)])

                def f_vev(a, qq=qq, pe_v=pe_v):
                    a.wait_ge(sems["PE"], pe_v)
                    for tl in range(4):
                        for oh in range(2):
                            ins = a.copy(
                                Vt[:, (4 * qq + tl) * DKV + oh * 512:
                                   (4 * qq + tl) * DKV + oh * 512 + 512],
                                bank(tl * 2 + oh))
                    inc(ins, "AC")
                block.scalar(f_vev)
                banks_free = [("AC", C["AC"]), ("DV", k_rope_dv)]

                if only_q0 and qq == 0:
                    dbg_dump("KT", KT[:], banks_free)
                    dbg_dump("V", Vt[:], banks_free)

                    def f_end(sync, st=None):
                        sync.wait_ge(sems["ST"], C["ST"])
                    block.sync(f_end)
                    return nc, dbg

            kv_ready[half] = [("DV", C["DV"]), ("AC", C["AC"])]
            if half == 1:
                dbg_dump("KT", KT[:], kv_ready[1])
                dbg_dump("V", Vt[:], kv_ready[1])

            for js in range(4):
                def fsq(sync, js=js, strip=strip, xw=list(x_free)):
                    wt(sync, xw)
                    wt(sync, sq_done())
                    r0 = strip * 512 + js * 128
                    inc(sync.dma_start(out=x_t[:], in_=x_q[r0:r0 + 128, :]),
                        "LX", 16)
                block.sync(fsq)
                ld_x = C["LX"]
                dv_xn = emit_ln(
                    x_t[:], xn_t[:],
                    dv_waits=[("LX", ld_x)],
                    ac_waits=[("PE", last_xn_pe)] + list(rope_free))
                x_free = [("DV", C["DV"]), ("AC", C["AC"])]
                tpack(lambda k: xn_t[:, k * 128:(k + 1) * 128], NCH,
                      lambda grp, cnt, js=js: ownT3[:, grp * 8:grp * 8 + cnt,
                                                    js * 128:(js + 1) * 128],
                      pe_waits=[("DV", dv_xn)] + list(banks_free),
                      evac_waits=[("PE", wo_done_pe)] + st_now())
                last_xn_pe = C["PE"]
            ownT_ready_ac = C["AC"]

            def f_mload(sync, xw=list(x_free), pw=last_xn_pe):
                wt(sync, xw + [("PE", pw)] + list(rope_free))
                inc(sync.dma_start(out=masksB[:], in_=masks_in), "MX", 16)
                inc(sync.dma_start(out=qcosB[:], in_=qcos_in), "MX", 16)
                inc(sync.dma_start(out=qsinB[:], in_=qsin_in), "MX", 16)
            block.sync(f_mload)
            ld_tbl = C["MX"]

            qrope_dv = 0
            for og in range(8):
                bs = (og % 2) * 4
                srcs = [s_wq[og * 512:(og + 1) * 512, d * 128:(d + 1) * 128]
                        for d in range(NCH)]

                def q_mm(t, i, w, bs=bs):
                    for oc in range(4):
                        ins = t.matmul(bank(bs + oc),
                                       w[:, oc * 128:(oc + 1) * 128],
                                       ownT[:, i * 512:(i + 1) * 512],
                                       start=(i == 0), stop=(i == NCH - 1))
                    return ins
                pw = ([("AC", ownT_ready_ac), ("MX", ld_tbl)]
                      + (list(banks_free) if og < 2
                         else [("DV", qrope_dv)]))
                sweep(srcs, q_mm, pe_waits=pw)
                pe_q = C["PE"]
                sl = slice(strip * 512, strip * 512 + 512)
                for oc in range(4):
                    h = og * 4 + oc
                    dbk = None
                    if strip == 0 and og == 0 and "Qraw" in dbg:
                        dbk = attnT[:, oc * 512:(oc + 1) * 512]
                    qrope_dv = rope_bank(
                        bs + oc, qcosB[:, sl], qsinB[:, sl],
                        QT[:, h * 512:(h + 1) * 512], rope_tmp,
                        dv_waits=([("PE", pe_q), ("MX", ld_tbl),
                                   ("SP2", rn_spilled.get(0, 0) if strip else 0)]
                                  if oc == 0 else [("MX", ld_tbl)]),
                        dbg_kf=dbk)
            qproj_done = [("DV", qrove_dv := qrope_dv), ("PE", C["PE"])]
            if strip == 0:
                if upto != "qproj":
                    stage_burst(["wo"], pe_wait=C["PE"])
                dbg_dump("QT0", QT[:], [("DV", qrope_dv)])
                if "Qraw" in dbg:
                    dbg_dump("Qraw", attnT[:, 0:4 * 512], [("AC", C["AC"])])
                if upto == "qproj":
                    if "swq" in dbg:
                        for c in range(4):
                            def fld(sync, c=c):
                                sync.wait_ge(sems["ST"], C["ST"])
                                inc(sync.dma_start(
                                    out=xn_t[:],
                                    in_=s_wq[c * 128:(c + 1) * 128, :]),
                                    "LX", 16)
                            block.sync(fld)
                            lxc = C["LX"]

                            def fst(sync, c=c, lxc=lxc):
                                sync.wait_ge(sems["LX"], lxc)
                                inc(sync.dma_start(
                                    out=dbg["swq"][:, c * D:(c + 1) * D],
                                    in_=xn_t[:]), "ST", 16)
                            block.sync(fst)
                    def f_endq(sync):
                        sync.wait_ge(sems["ST"], C["ST"])
                    block.sync(f_endq)
                    return nc, dbg
            else:
                stage_burst(["gw"], pe_wait=C["PE"])

            NG = 4 * NH
            exp_ac = [0] * NG
            sm_dv = [0] * NG
            av_dv = [0] * NG

            def emit_scores(gi):
                js, h = divmod(gi, NH)
                jown = strip * 4 + js
                base = (gi % 2) * 2048
                hkv = h // 4

                def ft(t):
                    if gi < 2:
                        wt(t, qproj_done + kv_ready[strip])
                    else:
                        t.wait_ge(sems["AC"], exp_ac[gi - 2])
                        t.wait_ge(sems["DV"], av_dv[gi - 2])
                    lhs = QT[:, h * 512 + js * 128: h * 512 + (js + 1) * 128]
                    n512 = (256 * jown) // 512
                    ins = None
                    for c in range(n512):
                        ins = t.matmul(PS[:, base + c * 512: base + (c + 1) * 512],
                                       lhs, KT[:, hkv * T + c * 512:
                                               hkv * T + (c + 1) * 512],
                                       start=True, stop=True)
                    if jown % 2 == 1:
                        o = n512 * 512
                        ins = t.matmul(PS[:, base + o: base + o + 256],
                                       lhs, KT[:, hkv * T + o: hkv * T + o + 256],
                                       start=True, stop=True)
                    o = 256 * jown
                    ins = t.matmul(PS[:, base + o: base + o + 256],
                                   lhs, KT[:, hkv * T + o: hkv * T + o + 256],
                                   start=True, stop=True)
                    inc(ins, "PE")
                block.tensor(ft)
                return C["PE"]

            def emit_softmax(gi, pe_s):
                js, h = divmod(gi, NH)
                jown = strip * 4 + js
                Kj = 256 * (jown + 1)
                base = (gi % 2) * 2048
                par = gi % 2
                sco = PS[:, base: base + Kj]

                def fv(v):
                    if gi == 0:
                        wt(v, qproj_done + [("MX", ld_tbl)] + st_now())
                    v.wait_ge(sems["PE"], pe_s)
                    inc(v.tensor_tensor(PS[:, base + Kj - 256: base + Kj],
                                        PS[:, base + Kj - 256: base + Kj],
                                        masksB[:, jown * 256:(jown + 1) * 256],
                                        ALU.add), "DV")
                    v.wait_ge(sems["DV"], C["DV"])
                    inc(v.reduce_max(nm2[:, par:par + 1], sco, axis=AX.X,
                                     negate=True), "DV")
                block.vector(fv)
                dv_m = C["DV"]

                def fa(a):
                    a.wait_ge(sems["DV"], dv_m)
                    inc(a.activation(p0B[:, par * T: par * T + Kj], sco, AF.Exp,
                                     bias=nm2[:, par:par + 1], scale=1.0,
                                     accum_out=sm2[:, par:par + 1]), "AC")
                    exp_ac[gi] = C["AC"]
                    a.wait_ge(sems["AC"], C["AC"])
                    inc(a.activation(rr2[:, par:par + 1], sm2[:, par:par + 1],
                                     AF.Ln, bias=0.0, scale=1.0), "AC")
                    a.wait_ge(sems["AC"], C["AC"])
                    inc(a.activation(rr2[:, par:par + 1], rr2[:, par:par + 1],
                                     AF.Exp, bias=0.0, scale=-1.0), "AC")
                block.scalar(fa)
                ac_r = C["AC"]

                def fv2(v):
                    v.wait_ge(sems["AC"], ac_r)
                    inc(v.tensor_scalar_mul(pB[:, par * T: par * T + Kj],
                                            p0B[:, par * T: par * T + Kj],
                                            rr2[:, par:par + 1]), "DV")
                block.vector(fv2)
                sm_dv[gi] = C["DV"]

            def emit_ptav(gi):
                js, h = divmod(gi, NH)
                jown = strip * 4 + js
                nblk = 2 * (jown + 1)
                bs = (gi % 2) * 4
                par = gi % 2
                hkv = h // 4
                na = min(8, nblk)
                nb = nblk - na
                av = PS[:, bs * 512 + 1024: bs * 512 + 1152]

                def ft1(t):
                    t.wait_ge(sems["DV"], sm_dv[gi])
                    for k in range(na):
                        ins = t.transpose(bank_bf(bs)[:, k * 128:(k + 1) * 128],
                                          pB[:, par * T + k * 128:
                                             par * T + (k + 1) * 128], ident[:])
                    if nb:
                        for k in range(nb):
                            ins = t.transpose(
                                bank_bf(bs + 1)[:, k * 128:(k + 1) * 128],
                                pB[:, par * T + (8 + k) * 128:
                                   par * T + (9 + k) * 128], ident[:])
                    inc(ins, "PE")
                block.tensor(ft1)
                pe_pt = C["PE"]

                def fa(a):
                    a.wait_ge(sems["PE"], pe_pt)
                    inc(a.copy(ptB[:, par * T: par * T + na * 128],
                               bank_bf(bs)[:, 0:na * 128]), "AC")
                block.scalar(fa)
                ac_ea = C["AC"]
                dv_eb = 0
                if nb:
                    def fveb(v):
                        v.wait_ge(sems["PE"], pe_pt)
                        inc(v.tensor_copy(
                            ptB[:, par * T + 1024: par * T + 1024 + nb * 128],
                            bank_bf(bs + 1)[:, 0:nb * 128]), "DV")
                    block.vector(fveb)
                    dv_eb = C["DV"]

                def ft3(t):
                    t.wait_ge(sems["AC"], ac_ea)
                    if nb:
                        t.wait_ge(sems["DV"], dv_eb)
                    for kb in range(nblk):
                        ins = t.matmul(
                            av,
                            Vt[:, kb * DKV + hkv * 128: kb * DKV + (hkv + 1) * 128],
                            ptB[:, par * T + kb * 128: par * T + (kb + 1) * 128],
                            start=(kb == 0), stop=(kb == nblk - 1))
                    inc(ins, "PE")
                block.tensor(ft3)
                pe_av = C["PE"]

                def fvav(v):
                    v.wait_ge(sems["PE"], pe_av)
                    inc(v.tensor_copy(
                        attnT[:, h * 512 + js * 128: h * 512 + (js + 1) * 128],
                        av), "DV")
                block.vector(fvav)
                av_dv[gi] = C["DV"]

            for gi in range(NG + 1):
                if gi < NG:
                    pe_s = emit_scores(gi)
                    emit_softmax(gi, pe_s)
                if gi >= 1:
                    emit_ptav(gi - 1)
            attn_end = [("DV", C["DV"]), ("PE", C["PE"]), ("AC", C["AC"])]
            if strip == 0:
                dbg_dump("attnT0", attnT[:], attn_end)

            wo_evac_ac = 0
            for og in range(8):
                bs = (og % 2) * 4
                srcs = [s_wo[og * 512:(og + 1) * 512, i * 128:(i + 1) * 128]
                        for i in range(NCH)]

                def wo_mm(t, i, w, bs=bs):
                    for oc in range(4):
                        ins = t.matmul(bank(bs + oc),
                                       w[:, oc * 128:(oc + 1) * 128],
                                       attnT[:, i * 512:(i + 1) * 512],
                                       start=(i == 0), stop=(i == NCH - 1))
                    return ins
                pw = list(attn_end) if og < 2 else [("AC", wo_evac_ac)]
                sweep(srcs, wo_mm, pe_waits=pw)
                pe_wo = C["PE"]

                def f_woev(a, og=og, bs=bs, pe_wo=pe_wo):
                    a.wait_ge(sems["PE"], pe_wo)
                    for oc in range(4):
                        ins = a.copy(poC[:, (og * 4 + oc) * 512:
                                         (og * 4 + oc + 1) * 512], bank(bs + oc))
                    inc(ins, "AC")
                block.scalar(f_woev)
                wo_evac_ac = C["AC"]

            g_free = list(attn_end)
            for js in range(4):
                def f_xq(sync, js=js, strip=strip, gfw=list(g_free)):
                    wt(sync, gfw + [("AC", wo_evac_ac)])
                    r0 = strip * 512 + js * 128
                    inc(sync.dma_start(out=xqC[:], in_=x_q[r0:r0 + 128, :]),
                        "LX", 16)
                block.sync(f_xq)
                ld_xq = C["LX"]

                ac_pt = tpack(
                    lambda k, js=js: poC3[:, k, js * 128:(js + 1) * 128], NCH,
                    lambda grp, cnt: ptokC[:, grp * 1024: grp * 1024 + cnt * 128],
                    pe_waits=[("AC", wo_evac_ac)],
                    evac_waits=list(g_free))

                def f_radd(v, ld_xq=ld_xq, ac_pt=ac_pt):
                    v.wait_ge(sems["LX"], ld_xq)
                    v.wait_ge(sems["AC"], ac_pt)
                    inc(v.tensor_tensor(ptokC[:], ptokC[:], xqC[:], ALU.add),
                        "DV")
                block.vector(f_radd)
                dv_r = C["DV"]

                def f_rsp(sync, js=js, strip=strip, dv_r=dv_r):
                    sync.wait_ge(sems["DV"], dv_r)
                    r0 = strip * 512 + js * 128
                    inc(sync.dma_start(out=y[r0:r0 + 128, :], in_=ptokC[:]),
                        "ST", 16)
                block.sync(f_rsp)

                dv_rn = emit_ln(ptokC[:], rnC[:], dv_waits=[("DV", dv_r)],
                                ac_waits=[("DV", dv_r)])
                tpack(lambda k: rnC[:, k * 128:(k + 1) * 128], NCH,
                      lambda grp, cnt, js=js: rnTs3[:, grp * 8:grp * 8 + cnt,
                                                    js * 128:(js + 1) * 128],
                      pe_waits=[("DV", dv_rn)],
                      evac_waits=(list(attn_end) + st_now() +
                                  ([("SP2", rn_spilled.get(0, 0))]
                                   if strip == 1 else [])))
                g_free = [("DV", C["DV"]), ("PE", C["PE"]), ("AC", C["AC"])]

            wo_done_pe = C["PE"]
            if strip == 1:
                stage_burst(["uw", "dw"], pe_wait=C["PE"])
            def f_rnsp(sc, strip=strip, gfw=list(g_free)):
                wt(sc, gfw)
                dst = rnT_sp.rearrange("p (c w) -> p c w", w=OWN)[
                    :, :, strip * 512:(strip + 1) * 512]
                inc(sc.dma_start(out=dst, in_=rnTs3), "SP2", 16)
            block.scalar(f_rnsp)
            rn_spilled[strip] = C["SP2"]
            r_write_st = C["ST"]
            x_free = list(g_free)
            banks_free = [("AC", g_free[2][1]), ("DV", g_free[0][1])]
            c_done = list(g_free)

        def f_rnld(sync, w=rn_spilled[1], cw=list(c_done) + st_now()):
            sync.wait_ge(sems["SP2"], w)
            wt(sync, cw)
            inc(sync.dma_start(out=rnT[:], in_=rnT_sp), "RB", 16)
        block.sync(f_rnld)
        ld_rnT = C["RB"]

        gu_ready = {}
        d1_spill = {}
        ffn_done_dv = 0
        dn_dv_end = 0
        icgs = []
        ic0 = 0
        while ic0 < NIC:
            icn = min(4, NIC - ic0)
            icgs.append((ic0, icn))
            ic0 += icn

        for ih in range(2):
            prev_up_dv = 0
            for (ic0, icn) in icgs:
                rows = slice(ih * IHALF + ic0 * 128,
                             ih * IHALF + (ic0 + icn) * 128)
                srcs_g = [s_gw[rows, d * 128:(d + 1) * 128] for d in range(NCH)]
                srcs_u = [s_uw[rows, d * 128:(d + 1) * 128] for d in range(NCH)]

                def gu_mm(t, i, w, icn=icn):
                    for ic in range(icn):
                        for ts in range(2):
                            ins = t.matmul(bank(ic * 2 + ts),
                                           w[:, ic * 128:(ic + 1) * 128],
                                           rnT[:, i * OWN + ts * 512:
                                               i * OWN + (ts + 1) * 512],
                                           start=(i == 0), stop=(i == NCH - 1))
                    return ins
                pw = [("RB", ld_rnT)]
                if prev_up_dv:
                    pw.append(("DV", prev_up_dv))
                if ic0 == 0:
                    pw += st_now()
                    if ih == 0:
                        pw += list(c_done)
                    else:
                        pw.append(("DV", dn_dv_end))
                sweep(srcs_g, gu_mm, pe_waits=pw, width=icn * 128)
                pe_g = C["PE"]

                def f_gev(a, ic0=ic0, icn=icn, pe_g=pe_g, first=(ih == 0 and ic0 == 0)):
                    if first:
                        wt(a, c_done + [("SP2", rn_spilled[1])])
                    a.wait_ge(sems["PE"], pe_g)
                    for ic in range(icn):
                        for ts in range(2):
                            ins = a.activation(
                                gu[:, (ic0 + ic) * OWN + ts * 512:
                                   (ic0 + ic) * OWN + (ts + 1) * 512],
                                bank(ic * 2 + ts), AF.Silu,
                                bias=0.0, scale=gs_b[:])
                    inc(ins, "AC")
                block.scalar(f_gev)
                ac_gev = C["AC"]

                sweep(srcs_u, gu_mm, pe_waits=[("AC", ac_gev)],
                      width=icn * 128)
                pe_u = C["PE"]

                def f_uev(v, ic0=ic0, icn=icn, pe_u=pe_u):
                    v.wait_ge(sems["PE"], pe_u)
                    for ic in range(icn):
                        for ts in range(2):
                            sl = slice((ic0 + ic) * OWN + ts * 512,
                                       (ic0 + ic) * OWN + (ts + 1) * 512)
                            ins = v.scalar_tensor_tensor(
                                gu[:, sl], bank(ic * 2 + ts), us_b[:],
                                gu[:, sl], op0=ALU.mult, op1=ALU.mult)
                    inc(ins, "DV")
                block.vector(f_uev)
                prev_up_dv = C["DV"]
            gu_ready[ih] = C["DV"]
            if ih == 0:
                dbg_dump("gu0", gu[:], [("DV", gu_ready[0])])
                dbg_dump("rnT", rnT[:], [("RB", ld_rnT), ("DV", gu_ready[0])])

            prev_dn_dv = 0
            for og in range(8):
                srcs = [s_dw[og * 512:(og + 1) * 512,
                             ih * IHALF + ic * 128: ih * IHALF + (ic + 1) * 128]
                        for ic in range(NIC)]

                def d_mm(t, i, w):
                    for oc in range(4):
                        for ts in range(2):
                            ins = t.matmul(bank(oc * 2 + ts),
                                           w[:, oc * 128:(oc + 1) * 128],
                                           gu[:, i * OWN + ts * 512:
                                               i * OWN + (ts + 1) * 512],
                                           start=(i == 0), stop=(i == NIC - 1))
                    return ins
                sweep(srcs, d_mm,
                      pe_waits=[("DV", max(gu_ready[ih], prev_dn_dv))])
                pe_d = C["PE"]

                if ih == 0:
                    sp2_prev = d1_spill.get("total", 0)
                    for k in range(8):
                        oc, ts = divmod(k, 2)
                        slot = k % 4
                        col = (og * 4 + oc) * OWN + ts * 512
                        sgw = sp2_prev if k < 4 else d1_spill.get("half", 0)

                        def f_dev(v, oc=oc, ts=ts, slot=slot, pe_d=pe_d,
                                  sgw=sgw):
                            v.wait_ge(sems["PE"], pe_d)
                            if sgw:
                                v.wait_ge(sems["SP2"], sgw)
                            inc(v.tensor_scalar_mul(
                                d1b[slot][:], bank(oc * 2 + ts), ds_b[:]), "DV")
                        block.vector(f_dev)
                        dv_d = C["DV"]

                        def f_dsp(sc, slot=slot, col=col, dv_d=dv_d):
                            sc.wait_ge(sems["DV"], dv_d)
                            inc(sc.dma_start(out=d1_sp[:, col:col + 512],
                                             in_=d1b[slot][:]), "SP2", 16)
                        block.scalar(f_dsp)
                        if k == 3:
                            d1_spill["half"] = C["SP2"]
                    d1_spill["total"] = C["SP2"]
                    prev_dn_dv = C["DV"]
                else:
                    for pair in range(2):
                        dv_before = C["DV"]

                        def f_rb(sync, og=og, pair=pair, dv_before=dv_before,
                                 first=(og == 0 and pair == 0)):
                            if first:
                                sync.wait_ge(sems["SP2"],
                                             d1_spill["total"])
                            wt(sync, sq_done())
                            sync.wait_ge(sems["DV"], dv_before)
                            for k in range(4):
                                oc = (pair * 4 + k) // 2
                                ts = (pair * 4 + k) % 2
                                col = (og * 4 + oc) * OWN + ts * 512
                                inc(sync.dma_start(
                                    out=d1b[k][:],
                                    in_=d1_sp[:, col:col + 512]), "RB", 16)
                        block.sync(f_rb)
                        ld_rb = C["RB"]

                        def f_cmb(v, og=og, pair=pair, pe_d=pe_d, ld_rb=ld_rb,
                                  stw=(st_now() if og == 0 and pair == 0 else [])):
                            wt(v, stw)
                            v.wait_ge(sems["PE"], pe_d)
                            v.wait_ge(sems["RB"], ld_rb)
                            for k in range(4):
                                oc = (pair * 4 + k) // 2
                                ts = (pair * 4 + k) % 2
                                col = (og * 4 + oc) * OWN + ts * 512
                                ins = v.scalar_tensor_tensor(
                                    ffnT[:, col:col + 512],
                                    bank(oc * 2 + ts), ds_b[:], d1b[k][:],
                                    op0=ALU.mult, op1=ALU.add)
                            inc(ins, "DV")
                        block.vector(f_cmb)
                    prev_dn_dv = C["DV"]
            dn_dv_end = C["DV"]
            if ih == 1:
                ffn_done_dv = C["DV"]

        ffnT3 = ffnT[:].rearrange("p (c w) -> p c w", w=OWN)
        prev_tile_dv = ffn_done_dv
        prev_y_st = 0
        for tl in range(8):
            def f_rf(sync, tl=tl, dvw=prev_tile_dv):
                sync.wait_ge(sems["DV"], dvw)
                sync.wait_ge(sems["ST"], r_write_st)
                inc(sync.dma_start(out=rF[:], in_=y[tl * 128:(tl + 1) * 128, :]),
                    "RB", 16)
            block.sync(f_rf)
            ld_r = C["RB"]

            for grp in range(4):
                bb = 6 + (grp % 2)

                def ftf(t, tl=tl, grp=grp, bb=bb, bw=tp_hist[6 + (grp % 2)],
                        fw=ffn_done_dv if (tl == 0 and grp == 0) else 0):
                    if fw:
                        t.wait_ge(sems["DV"], fw)
                    wt(t, [bw])
                    for k in range(8):
                        c = grp * 8 + k
                        ins = t.transpose(bank_bf(bb)[:, k * 128:(k + 1) * 128],
                                          ffnT3[:, c, tl * 128:(tl + 1) * 128],
                                          ident[:])
                    inc(ins, "PE")
                block.tensor(ftf)
                pe_f = C["PE"]

                def fvf(v, grp=grp, bb=bb, pe_f=pe_f, ld_r=ld_r,
                        yw=prev_y_st):
                    v.wait_ge(sems["PE"], pe_f)
                    v.wait_ge(sems["RB"], ld_r)
                    if yw:
                        v.wait_ge(sems["ST"], yw)
                    inc(v.tensor_tensor(yF[:, grp * 1024:(grp + 1) * 1024],
                                        bank_bf(bb)[:, 0:1024],
                                        rF[:, grp * 1024:(grp + 1) * 1024],
                                        ALU.add), "DV")
                block.vector(fvf)
                tp_hist[bb] = ("DV", C["DV"])
            prev_tile_dv = C["DV"]

            def f_yw(sync, tl=tl, dv_y=prev_tile_dv):
                sync.wait_ge(sems["DV"], dv_y)
                inc(sync.dma_start(out=y[tl * 128:(tl + 1) * 128, :], in_=yF[:]),
                    "ST", 16)
            block.sync(f_yw)
            prev_y_st = C["ST"]

        def f_fin(sync, st=C["ST"]):
            sync.wait_ge(sems["ST"], st)
        block.sync(f_fin)

    return nc, dbg


_CACHE = {}


def _rope_tables():
    if "rope" in _CACHE:
        return _CACHE["rope"]
    ROPE_THETA = 500000.0
    factor, low_f, high_f, old_ctx = 32.0, 1.0, 4.0, 8192
    inv = 1.0 / ROPE_THETA ** (np.arange(0, HD, 2, dtype=np.float32) / HD)
    low_wl, high_wl = old_ctx / low_f, old_ctx / high_f
    wavelen = 2 * math.pi / inv
    inv_l = np.where(wavelen > low_wl, inv / factor, inv)
    smooth = (old_ctx / wavelen - low_f) / (high_f - low_f)
    smoothed = (1 - smooth) * inv_l / factor + smooth * inv_l
    is_med = ~(wavelen < high_wl) & ~(wavelen > low_wl)
    invf = np.where(is_med, smoothed, inv_l).astype(np.float32)
    pos = np.arange(T, dtype=np.float32)
    freqs = pos[:, None] * invf[None, :]
    emb = np.concatenate([freqs, freqs], axis=-1)
    cosT = np.cos(emb).astype(np.float32).T.copy()
    sinT = np.sin(emb).astype(np.float32).T.copy()
    sinS = np.concatenate([-sinT[0:64], sinT[64:128]], axis=0)
    _CACHE["rope"] = (cosT, sinS)
    return cosT, sinS


def _get_nc():
    if "nc" not in _CACHE:
        _CACHE["nc"] = build_program()[0]
    return _CACHE["nc"]


def _own_rows(par):
    return np.concatenate([np.arange((2 * j + par) * 128,
                                     (2 * j + par + 1) * 128) for j in range(8)])


def kernel(L0_attn_x, L0_attn_Wq, L0_attn_Wk, L0_attn_Wv, L0_attn_Wo,
           L0_ffn_gate_w, L0_ffn_gate_s, L0_ffn_up_w, L0_ffn_up_s,
           L0_ffn_down_w, L0_ffn_down_s):
    x = np.ascontiguousarray(np.asarray(L0_attn_x, dtype=np.float32))
    cosT, sinS = _rope_tables()
    scale = 1.0 / math.sqrt(HD)
    ident = np.eye(128, dtype=np.float32).astype(bf16)
    masks = np.zeros((2, 128, 8 * 256), np.float32)
    qc = np.zeros((2, 128, OWN), np.float32)
    qs = np.zeros((2, 128, OWN), np.float32)
    for par in range(2):
        tq = np.arange(128)[:, None]
        kk = np.arange(256)[None, :]
        m = np.where(kk <= 128 * par + tq, 0.0, -1e9).astype(np.float32)
        for j in range(8):
            masks[par][:, j * 256:(j + 1) * 256] = m
        rows = _own_rows(par)
        qc[par] = cosT[:, rows] * scale
        qs[par] = sinS[:, rows] * scale

    common = {
        "wq": np.ascontiguousarray(L0_attn_Wq, dtype=np.float32),
        "wk": np.ascontiguousarray(L0_attn_Wk, dtype=np.float32),
        "wv": np.ascontiguousarray(L0_attn_Wv, dtype=np.float32),
        "wo": np.ascontiguousarray(L0_attn_Wo, dtype=np.float32),
        "gw": np.ascontiguousarray(L0_ffn_gate_w, dtype=np.int32),
        "uw": np.ascontiguousarray(L0_ffn_up_w, dtype=np.int32),
        "dw": np.ascontiguousarray(L0_ffn_down_w, dtype=np.int32),
        "gs_in": np.asarray(L0_ffn_gate_s, np.float32).reshape(1, 1),
        "us_in": np.asarray(L0_ffn_up_s, np.float32).reshape(1, 1),
        "ds_in": np.asarray(L0_ffn_down_s, np.float32).reshape(1, 1),
        "kcos_in": cosT.astype(bf16),
        "ksin_in": sinS.astype(bf16),
        "ident_in": ident,
        "swp_in": np.roll(np.eye(128, dtype=np.float32), 64, axis=0).astype(bf16),
    }
    in_maps = []
    for core in range(8):
        b, par = core // 2, core % 2
        m = dict(common)
        m["x_full"] = x[b]
        m["x_q"] = np.ascontiguousarray(x[b][_own_rows(par)])
        m["masks_in"] = masks[par]
        m["qcos_in"] = qc[par].astype(bf16)
        m["qsin_in"] = qs[par].astype(bf16)
        in_maps.append(m)

    nc = _get_nc()
    res = run_bass_kernel_spmd(nc, in_maps, core_ids=list(range(8)))
    out = np.zeros((4, T, D), np.float32)
    for core in range(8):
        b, par = core // 2, core % 2
        out[b][_own_rows(par)] = res.results[core]["y"]
    return out
